# version 27
# baseline (speedup 1.0000x reference)
import numpy as np
import ml_dtypes
from contextlib import ExitStack
import concourse.bass as bass
import concourse.mybir as mybir
from concourse.bass_utils import run_bass_kernel_spmd

F32 = mybir.dt.float32
BF16 = mybir.dt.bfloat16
AF = mybir.ActivationFunctionType
ALU = mybir.AluOpType
BFD = ml_dtypes.bfloat16
AX = mybir.AxisListType.X

D = 2048
S = 16
MT = 16
TOK = 1152
OWN = 1024
NT = 9
NCH = 3
CHW = 384
EPS = 1e-5


class Plan:
    ENGINES = ("tensor", "vector", "scalar", "gpsimd", "sync")

    def __init__(self):
        self.ops = {e: [] for e in self.ENGINES}
        self.cnt = {}
        self.dma_idx = {}

    def val(self, sem):
        return self.cnt.get(sem, 0)

    def add(self, eng, fn, waits=(), inc=None, amt=1):
        rec = None
        if inc is not None:
            self.cnt[inc] = self.cnt.get(inc, 0) + amt
            rec = (inc, amt)
            if amt == 16:
                self.dma_idx[id(fn)] = self.cnt[inc] // 16 - 1
        self.ops[eng].append((fn, tuple(w for w in waits if w and w[1]), rec))
        return self.val(inc) if inc is not None else None


def build_program():
    nc = bass.Bass("TRN2", target_bir_lowering=False)
    es = ExitStack()

    def sb(name, shape, dt):
        return es.enter_context(nc.sbuf_tensor(name, shape, dt))

    def dram(name, shape, dt, kind):
        return nc.dram_tensor(name, shape, dt, kind=kind).ap()

    xin = dram("xin", [TOK, D], F32, "ExternalInput")
    WiT = dram("WiT", [D, D], F32, "ExternalInput")
    WoT = dram("WoT", [D, D], F32, "ExternalInput")
    Wfn = dram("Wfn", [D, D], F32, "ExternalInput")
    Arr = dram("Arr", [128, 256], F32, "ExternalInput")
    BwT = dram("BwT", [128, 256], F32, "ExternalInput")
    CwT = dram("CwT", [S, D], F32, "ExternalInput")
    binc = dram("binc", [128, 16], F32, "ExternalInput")
    Dcol = dram("Dcol", [128, 16], F32, "ExternalInput")
    borow = dram("borow", [1, D], F32, "ExternalInput")
    bfrow = dram("bfrow", [1, D], F32, "ExternalInput")
    E1in = dram("E1in", [128, MT * 128], BF16, "ExternalInput")
    E2in = dram("E2in", [128, MT * 128], BF16, "ExternalInput")
    SELin = dram("SELin", [128, 512], BF16, "ExternalInput")
    Iin = dram("Iin", [128, 128], BF16, "ExternalInput")
    MSKin = dram("MSKin", [128, 9], F32, "ExternalInput")
    yout = dram("yout", [OWN, D], F32, "ExternalOutput")
    scrx = dram("scrx", [TOK, D], BF16, "Internal")
    scr2 = dram("scr2", [OWN, D], BF16, "Internal")
    scrq = dram("scrq", [D, D], BF16, "Internal")

    AR1 = sb("AR1", [128, 18432], BF16)
    AR2 = sb("AR2", [128, 18432], BF16)
    AR3 = sb("AR3", [128, 16384], BF16)
    AR4 = sb("AR4", [128, 17664], BF16)
    WB = sb("WB", [128, 4096], BF16)
    QN = sb("QN", [128, 2048], BF16)
    X2D = sb("X2D", [128, 6144], F32)
    XF = sb("XF", [128, 2048], F32)
    FS = sb("FS", [128, 2048], F32)
    PZ = sb("PZ", [128, 2048], F32)
    XN1S = sb("XN1S", [128, 4096], BF16)

    E1c = sb("E1c", [128, 128], BF16)
    bnb = sb("bnb", [128, 96], F32)
    bno = sb("bno", [128, 8], F32)
    tc1 = sb("tc1", [128, 4], F32)
    r1c = sb("r1c", [128, 9], F32)
    m1c = sb("m1c", [128, 9], F32)
    rc2 = sb("rc2", [128, 4], F32)
    mc2 = sb("mc2", [128, 4], F32)
    thrp = sb("thrp", [128, 16], F32)
    thrn = sb("thrn", [128, 16], F32)
    bincl = sb("bincl", [128, 16], F32)
    Dcl = sb("Dcl", [128, 16], F32)
    mskc = sb("mskc", [128, 9], F32)
    epsc = sb("epsc", [128, 1], F32)
    psb = [es.enter_context(nc.psum_tensor(f"psb{i}", [128, 512], F32))
           for i in range(8)]

    xn1T = AR1
    xprT = AR2
    yT = AR3
    Psi = AR4[:, 0:8192]
    E1T = AR4[:, 8192:10240]
    E2T = AR4[:, 10240:12288]
    SEL = AR4[:, 12288:12800]
    dtp = AR4[:, 12800:13952]
    GTp = AR4[:, 13952:15104]
    wrpv = [AR4[:, 15104:15616], AR4[:, 17152:17664]]
    phit = [AR4[:, 15616 + i * 512: 15616 + (i + 1) * 512] for i in range(3)]
    AR4f = AR4[:].bitcast(F32)
    WBf = WB[:].bitcast(F32)
    WoA = AR1[:, 0:16384]
    WoB = AR2[:, 0:16384]
    qTa = AR1[:, 0:16384]
    qTb = AR2[:, 0:16384]
    x2nT = AR3
    bcv = QN
    X2Db = X2D[:].bitcast(BF16)
    CwTp = X2Db[:, 0:2048]
    qtmv = [X2Db[:, 8192:10240], X2Db[:, 10240:12288]]
    dtsv = [X2D[32 * c:32 * c + 16, 2048:2432] for c in range(3)]
    dts2v = [X2D[32 * c:32 * c + 16, 2432:2816] for c in range(3)]
    Ibf = X2Db[:, 5632:5760]
    ddgr = X2Db[:, 5760:6016]
    BwTs = X2Db[:, 6016:6272]
    z2f = PZ[:, 0:2048]
    XF2 = FS

    def x2sl(tt):
        if tt < 4:
            return AR4f[:, tt * 2048:(tt + 1) * 2048]
        if tt == 4:
            return WBf[:, 0:2048]
        return X2D[:, (tt - 5) * 2048:(tt - 4) * 2048]

    sems = {}
    for sname in ("t", "v", "s", "g"):
        sems[sname] = es.enter_context(nc.semaphore(sname))
    NRING = 8
    dma_rings = {}
    for qname in ("dg", "dh", "dsc"):
        dma_rings[qname] = [es.enter_context(nc.semaphore(f"{qname}{i}"))
                            for i in range(NRING)]

    P = Plan()

    def cover(q, vals):
        best = {}
        for v in vals:
            k = v // 16 - 1
            r = k % 8
            best[r] = max(best.get(r, 0), v)
        return [(q, v) for v in best.values()]

    def DG(fn, waits=()):
        return P.add("gpsimd", fn, waits, "dg", 16)

    def DH(fn, waits=()):
        return P.add("sync", fn, waits, "dh", 16)

    def DS(fn, waits=()):
        return P.add("scalar", fn, waits, "dsc", 16)

    def T(fn, waits=(), inc=True):
        return P.add("tensor", fn, waits, "t" if inc else None)

    def V(fn, waits=(), inc=True):
        return P.add("vector", fn, waits, "v" if inc else None)

    def SS(fn, waits=(), inc=True):
        return P.add("scalar", fn, waits, "s" if inc else None)

    def G(fn, waits=(), inc=True):
        return P.add("gpsimd", fn, waits, "g" if inc else None)

    c_e1 = DG(lambda g: g.dma_start(out=E1T, in_=E1in[:]))
    c_e2 = DG(lambda g: g.dma_start(out=E2T, in_=E2in[:]))
    c_sel = DG(lambda g: g.dma_start(out=SEL, in_=SELin[:]))
    c_i = DG(lambda g: g.dma_start(out=Ibf, in_=Iin[:]))
    c_msk = DG(lambda g: g.dma_start(out=mskc[:], in_=MSKin[:]))
    c_a = DG(lambda g: g.dma_start(out=Af, in_=Arr[:]))
    c_bw = DG(lambda g: g.dma_start(out=BwTs, in_=BwT[:]))
    c_dc = DG(lambda g: g.dma_start(out=Dcl[:], in_=Dcol[:]))
    c_bi = DG(lambda g: g.dma_start(out=bincl[:], in_=binc[:]))
    g_eps = G(lambda g: g.memset(epsc[:], EPS))
    g_z1 = G(lambda g: g.memset(CwTp[:], 0.0))
    g_e0 = G(lambda g: g.memset(E1c[:], 0.0))
    g_e1c = G(lambda g: g.memset(E1c[0:1, :], 1.0), waits=[("g", g_e0)])
    g_z2 = G(lambda g: g.memset(dtp, 0.0))
    g_z3 = G(lambda g: g.memset(GTp, 0.0))
    c_cw = DG(lambda g: g.dma_start(out=CwTp[0:16, :], in_=CwT[:]),
              waits=[("g", g_z1)])

    ln1_done = []
    ln1_slot_sr = {0: None, 1: None}
    sw_list = []
    psb6b = psb[6][:].bitcast(BF16)
    psb7b = psb[7][:].bitcast(BF16)
    psb4b = psb[4][:].bitcast(BF16)
    psb5b = psb[5][:].bitcast(BF16)
    xe = {}
    tr_last = {}

    def xfb(tt):
        return XF if tt % 2 == 0 else XF2

    def xslot(tt):
        i = tt % 3
        return QN[:, 0:2048] if i == 2 else XN1S[:, i * 2048:(i + 1) * 2048]

    def emit_ln1(tt):
        c_x = DH(lambda sy, tt=tt: sy.dma_start(
            out=xfb(tt)[:], in_=xin[tt * 128:(tt + 1) * 128, :]),
            waits=[("s", ln1_done[tt - 2])] if tt >= 2 else [])
        p_ = tt % 2
        vb = None
        for q4 in range(4):
            w = [("dh", c_x)]
            if q4 == 0 and ln1_slot_sr[p_] is not None:
                w.append(("s", ln1_slot_sr[p_]))
            vb = V(lambda v, tt=tt, q4=q4, p_=p_: v.bn_stats(
                out=bnb[:, p_ * 24 + q4 * 6: p_ * 24 + (q4 + 1) * 6],
                in_=xfb(tt)[:, q4 * 512:(q4 + 1) * 512]),
                waits=(w if q4 == 0 else ()))
        va = V(lambda v, p_=p_: v.bn_aggr(out=bno[:, p_ * 2:p_ * 2 + 2],
                                          in_=bnb[:, p_ * 24:(p_ + 1) * 24]),
               waits=[("v", vb)])
        sl = SS(lambda s, p_=p_: s.activation(
            out=tc1[:, p_:p_ + 1], in_=bno[:, p_ * 2 + 1:p_ * 2 + 2],
            func=AF.Ln, bias=epsc[:, 0:1]), waits=[("v", va), ("g", g_eps)])
        sr = SS(lambda s, tt=tt, p_=p_: s.activation(
            out=r1c[:, tt:tt + 1], in_=tc1[:, p_:p_ + 1],
            func=AF.Exp, scale=-0.5), waits=[("s", sl)])
        ln1_slot_sr[p_] = sr
        vv = V(lambda v, tt=tt, p_=p_: v.scalar_tensor_tensor(
            m1c[:, tt:tt + 1], bno[:, p_ * 2:p_ * 2 + 1], -1.0,
            r1c[:, tt:tt + 1], ALU.mult, ALU.mult), waits=[("s", sr)])
        sa_w = [("v", vv)]
        if tt >= 3:
            sa_w += [("dsc", sw_list[tt - 3])]
        sa = SS(lambda s, tt=tt: s.activation(
            out=xslot(tt), in_=xfb(tt)[:], func=AF.Identity,
            bias=m1c[:, tt:tt + 1], scale=r1c[:, tt:tt + 1]), waits=sa_w)
        ln1_done.append(sa)
        sw = DS(lambda s_, tt=tt: s_.dma_start(
            out=scrx[tt * 128:(tt + 1) * 128, :], in_=xslot(tt)),
            waits=[("s", sa)])
        sw_list.append(sw)
        if tt % 3 == 2:
            ch = tt // 3
            tread[ch] = DS(lambda s_, ch=ch: s_.dma_start(
                out=xn1T[:].rearrange("p (dt c) -> p dt c", c=TOK)[
                    :, :, ch * CHW:(ch + 1) * CHW],
                in_=scrx[ch * CHW:(ch + 1) * CHW, :].rearrange(
                    "r (dt p) -> r dt p", p=128),
                transpose=True), waits=cover("dsc", sw_list[3 * ch:3 * ch + 3]))

    q_state = {"stt": [], "scw": [], "vq": [], "ld": []}

    def emit_quant(ost):
        q = qtmv[ost % 2]
        w_war = ([("v", q_state["vq"][ost - 2])] if ost >= 2
                 else [("s", ln1_done[8])])
        c_s = DH(lambda sy, ost=ost: sy.dma_start(
            out=wfs_view(ost),
            in_=Wfn[ost * 128:(ost + 1) * 128, :]), waits=w_war)
        vr = V(lambda v, ost=ost: v.tensor_reduce(
            out=thrp[:, ost:ost + 1], in_=wfs_view(ost), axis=AX,
            op=ALU.add, apply_absolute_value=True), waits=[("dh", c_s)])
        vr = V(lambda v, ost=ost: v.tensor_scalar(
            thrp[:, ost:ost + 1], thrp[:, ost:ost + 1], 0.7 / 2048.0, None,
            ALU.mult), waits=[("v", vr)])
        vr = V(lambda v, ost=ost: v.tensor_scalar(
            thrn[:, ost:ost + 1], thrp[:, ost:ost + 1], -1.0, None, ALU.mult),
            waits=[("v", vr)])
        gq = V(lambda v, ost=ost, q=q: v.tensor_scalar(
            q, wfs_view(ost), thrn[:, ost:ost + 1], -1.0, ALU.is_lt,
            ALU.mult), waits=[("v", vr)] +
            ([("dg", q_state["scw"][ost - 2])] if ost >= 2 else []))
        vq = V(lambda v, ost=ost, q=q: v.scalar_tensor_tensor(
            q, wfs_view(ost), thrp[:, ost:ost + 1], q,
            ALU.is_gt, ALU.add), waits=[("v", gq)])
        q_state["stt_v"] = vq
        sw = DG(lambda g, ost=ost, q=q: g.dma_start(
            out=scrq[ost * 128:(ost + 1) * 128, :], in_=q),
            waits=[("v", vq)])
        q_state["stt"].append(gq)
        q_state["vq"].append(vq)
        q_state["scw"].append(sw)

    def wfs_view(ost):
        return FS[:] if ost % 2 == 0 else PZ[:]

    tread = {}
    for tt in range(9):
        emit_ln1(tt)
    m1_evac = {}
    m1_t = {}
    quant_emitted = 0
    wi_r = WiT.rearrange("(kt p) o -> p kt o", p=128)
    for ot in range(16):
        w_war = [("t", m1_t[(ot - 2, 2)])] if ot >= 2 else []
        c_w = DG(lambda g, ot=ot: g.dma_start(
            out=WB[:, (ot % 2) * 2048:(ot % 2 + 1) * 2048].rearrange(
                "p (kt o) -> p kt o", o=128),
            in_=wi_r[:, :, ot * 128:(ot + 1) * 128]), waits=w_war)
        for ch in range(NCH):
            bank = (ot % 2) * 3 + ch
            for kt in range(16):
                waits = []
                if kt == 0:
                    waits = [("dg", c_w), ("dsc", tread[ch])]
                    if (ot - 2, ch) in m1_evac:
                        waits.append(("s", m1_evac[(ot - 2, ch)]))
                tv = T(lambda t, ot=ot, kt=kt, ch=ch, bank=bank: t.matmul(
                    psb[bank][:, 0:CHW],
                    WB[:, (ot % 2) * 2048 + kt * 128:(ot % 2) * 2048 + (kt + 1) * 128],
                    xn1T[:, kt * TOK + ch * CHW: kt * TOK + (ch + 1) * CHW],
                    start=(kt == 0), stop=(kt == 15)),
                    waits=waits, inc=(kt == 15))
                if kt == 15:
                    m1_t[(ot, ch)] = tv
            m1_evac[(ot, ch)] = SS(lambda s, ot=ot, ch=ch, bank=bank: s.activation(
                out=xprT[:, ot * TOK + ch * CHW: ot * TOK + (ch + 1) * CHW],
                in_=psb[bank][:, 0:CHW], func=AF.Identity,
                bias=bincl[:, ot:ot + 1]),
                waits=[("t", m1_t[(ot, ch)]), ("dg", c_bi)])
        if ot >= 4 and quant_emitted < 16:
            emit_quant(quant_emitted)
            quant_emitted += 1
            if ot >= 9 and quant_emitted < 16:
                emit_quant(quant_emitted)
                quant_emitted += 1
    while quant_emitted < 16:
        emit_quant(quant_emitted)
        quant_emitted += 1
    m1_last_t = m1_t[(15, 2)]
    m1_last_s = m1_evac[(15, 2)]

    v0 = V(lambda v: v.tensor_mul(tAv, Af, Af), waits=[("dg", c_a)])
    v1 = V(lambda v: v.tensor_scalar(tBv, tAv, -1.0 / 2880.0, 1.0 / 192.0,
                                     ALU.mult, ALU.add), waits=[("v", v0)])
    v2 = V(lambda v: v.tensor_mul(tBv, tBv, tAv), waits=[("v", v1)])
    v3 = V(lambda v: v.tensor_scalar(tBv, tBv, -1.0 / 8.0, None, ALU.add),
           waits=[("v", v2)])
    v4 = V(lambda v: v.tensor_mul(tBv, tBv, tAv), waits=[("v", v3)])
    v5 = V(lambda v: v.scalar_tensor_tensor(psiv, Af, 0.5, tBv,
                                            ALU.mult, ALU.add), waits=[("v", v4)])
    s_beta = SS(lambda s: s.activation(out=betav, in_=psiv, func=AF.Exp,
                                       scale=128.0), waits=[("v", v5)])
    AR3f = AR3[:].bitcast(F32)
    Af = AR3f[:, 4096:4352]
    psiv = AR3f[:, 4352:4608]
    betav = AR3f[:, 4608:4864]
    tAv = AR3f[:, 4864:5120]
    tBv = AR3f[:, 5120:5376]

    def fsm(m):
        return AR3f[:, m * 256:(m + 1) * 256]

    ST4 = AR3f[:, 0:4096].rearrange("p (m dt s) -> p m dt s", m=MT, dt=16)
    pview = Psi.rearrange("p (dt r) -> p dt r", r=512)
    d1 = pview[:, :, 0:256].rearrange("p dt (m s) -> p m dt s", s=16)
    d2 = pview[:, :, 256:512].rearrange("p dt (m s) -> p m dt s", s=16)
    vm = V(lambda v: v.memset(fsm(0), 1.0), waits=[("v", v5)])
    vm = V(lambda v: v.tensor_copy(out=fsm(1), in_=psiv), waits=[("v", vm)])
    for m in range(2, MT):
        vm = V(lambda v, m=m: v.tensor_mul(fsm(m), fsm(m - 1), psiv),
               waits=[("v", vm)])
    vm = V(lambda v: v.tensor_copy(out=d1, in_=ST4), waits=[("v", vm)])
    for m in range(MT):
        vm = V(lambda v, m=m: v.tensor_mul(fsm(m), fsm(m), betav),
               waits=[("v", vm), ("s", s_beta)])
    vm = V(lambda v: v.tensor_copy(out=d2, in_=ST4), waits=[("v", vm)])
    v_psi = vm

    phi_v = {}
    g_v = {}
    wrep_v = {}
    st2_t = {}
    st2_prev_bank_v = {}
    y_evac = {}
    y_t_war = {}
    y_last_t = None
    vdg_last = None

    v_dt = {}
    sp_s4 = {}
    t_delta = {}
    for ch in range(NCH):
        bank = 6 if ch != 1 else 7
        waits = [("s", m1_last_s), ("dg", c_bw)]
        if ch == 2:
            waits.append(("s", sp_s4[0]))
        for dt_ in range(16):
            tv = T(lambda t, ch=ch, dt_=dt_, bank=bank: t.matmul(
                psb[bank][0:16, 0:CHW], BwTs[:, dt_ * 16:(dt_ + 1) * 16],
                xprT[:, dt_ * TOK + ch * CHW: dt_ * TOK + (ch + 1) * CHW],
                start=(dt_ == 0), stop=(dt_ == 15)),
                waits=(waits if dt_ == 0 else ()), inc=(dt_ == 15))
        t_delta[ch] = tv
        dts, dts2 = dtsv[ch], dts2v[ch]
        s1 = SS(lambda s, ch=ch, bank=bank, dts=dts: s.activation(
            out=dts, in_=psb[bank][0:16, 0:CHW], func=AF.Abs),
            waits=[("t", tv)])
        s2 = SS(lambda s, dts=dts: s.activation(out=dts, in_=dts, func=AF.Exp,
                                                scale=-1.0), waits=[("s", s1)])
        s3 = SS(lambda s, dts=dts: s.activation(out=dts, in_=dts, func=AF.Ln,
                                                bias=1.0), waits=[("s", s2)])
        s4 = SS(lambda s, ch=ch, bank=bank, dts2=dts2: s.activation(
            out=dts2, in_=psb[bank][0:16, 0:CHW], func=AF.Relu),
            waits=[("s", s3)])
        sp_s4[ch] = s4
        v_dt[ch] = V(lambda v, ch=ch, dts=dts, dts2=dts2: v.tensor_add(
            dtp[0:16, ch * CHW:(ch + 1) * CHW], dts, dts2),
            waits=[("s", s4), ("g", g_z2)])
    dts_prev = sp_s4[2]

    def emit_stage2(sc):
        bg = 4 + sc % 2
        waits0 = [("v", phi_v[sc]), ("v", phi_v[sc - 1]),
                  ("dg", c_e1), ("dg", c_e2)]
        if sc <= 2:
            waits0.append(("s", m1_last_s))
        if bg in st2_prev_bank_v:
            waits0.append(st2_prev_bank_v[bg])
        n = 0
        tv = None
        for blk in range(2):
            srcp = phit[(sc - 1) % 3] if blk == 0 else phit[sc % 3]
            ET = E2T if blk == 0 else E1T
            off = 256 if blk == 0 else 0
            for m in range(MT):
                tv = T(lambda t, srcp=srcp, ET=ET, off=off, m=m, bg=bg, n=n:
                       t.matmul(psb[bg][0:16, 0:128],
                                srcp[:, off + m * 16: off + (m + 1) * 16],
                                ET[:, m * 128:(m + 1) * 128],
                                start=(n == 0), stop=(n == 2 * MT - 1)),
                       waits=(waits0 if n == 0 else ()),
                       inc=(n == 2 * MT - 1))
                n += 1
        st2_t[sc] = tv
        g_v[sc] = V(lambda v, sc=sc, bg=bg: v.tensor_copy(
            out=GTp[0:16, sc * 128:(sc + 1) * 128],
            in_=psb[bg][0:16, 0:128]), waits=[("t", tv), ("g", g_z3)])
        st2_prev_bank_v[bg] = ("v", g_v[sc])

    def emit_y(ch):
        nonlocal_vals = {}
        t0 = ch * CHW if ch > 0 else 128
        n_y = (ch + 1) * CHW - t0
        sc_hi = ch * 3 + 2
        for dt_ in range(16):
            bank = 4 + dt_ % 2
            dslot = dt_ % 2
            dw = [("dg", c_i), ("dg", c_dc)]
            if (ch, dt_ - 2) in y_evac:
                dw.append(("t", y_t_war[dt_ - 2]))
            elif ch > 0:
                dw.append(("t", y_t_war[14 + dslot]))
            vdg = V(lambda v, dt_=dt_, dslot=dslot: v.tensor_scalar(
                ddgr[:, dslot * 128:(dslot + 1) * 128], Ibf,
                Dcl[:, dt_:dt_ + 1], None, ALU.mult), waits=dw)
            waits = [("v", g_v[sc_hi]), ("dg", c_cw)]
            if (ch, dt_ - 2) in y_evac:
                waits.append(("s", y_evac[(ch, dt_ - 2)]))
            elif bank in st2_prev_bank_v and dt_ < 2:
                waits.append(st2_prev_bank_v[bank])
            T(lambda t, dt_=dt_, bank=bank, t0=t0, n_y=n_y: t.matmul(
                psb[bank][:, 0:n_y], CwTp[:, dt_ * 128:(dt_ + 1) * 128],
                GTp[:, t0:t0 + n_y], start=True, stop=False),
                waits=waits, inc=False)
            tv = T(lambda t, dt_=dt_, bank=bank, t0=t0, n_y=n_y, dslot=dslot:
                   t.matmul(
                psb[bank][:, 0:n_y], ddgr[:, dslot * 128:(dslot + 1) * 128],
                xprT[:, dt_ * TOK + t0: dt_ * TOK + t0 + n_y],
                start=False, stop=True), waits=[("v", vdg)])
            y_t_war[dt_] = tv
            y_evac[(ch, dt_)] = SS(lambda s, dt_=dt_, bank=bank, t0=t0, n_y=n_y:
                                   s.activation(
                out=yT[:, dt_ * OWN + t0 - 128: dt_ * OWN + t0 - 128 + n_y],
                in_=psb[bank][:, 0:n_y], func=AF.Identity), waits=[("t", tv)])
        for bank in (4, 5):
            st2_prev_bank_v[bank] = ("s", y_evac[(ch, 14 + (bank - 4))])
        return vdg, tv

    for sc in range(9):
        ch = sc // 3
        bp, bw_ = sc % 2, 2 + sc % 2
        waits = [("v", v_dt[ch]), ("dg", c_sel), ("dg", c_msk)]
        if sc >= 2:
            waits.append(("v", phi_v[sc - 2]))
        t_wr = T(lambda t, sc=sc, bw_=bw_: t.matmul(
            psb[bw_][:, 0:512], dtp[:, sc * 128:(sc + 1) * 128], SEL,
            start=True, stop=True), waits=waits)
        wrep_v[sc] = V(lambda v, sc=sc, bw_=bw_: v.tensor_scalar(
            wrpv[sc % 2], psb[bw_][:, 0:512], mskc[:, sc:sc + 1], None,
            ALU.mult), waits=[("t", t_wr)])
        for dt_ in range(16):
            waits = []
            if dt_ == 0:
                waits = [("s", m1_last_s), ("v", v_psi)]
                if sc >= 2:
                    waits.append(("v", phi_v[sc - 2]))
            tv = T(lambda t, sc=sc, dt_=dt_, bp=bp: t.matmul(
                psb[bp][:, 0:512],
                xprT[:, dt_ * TOK + sc * 128: dt_ * TOK + (sc + 1) * 128],
                Psi[:, dt_ * 512:(dt_ + 1) * 512],
                start=(dt_ == 0), stop=(dt_ == 15)),
                waits=waits, inc=(dt_ == 15))
        phi_v[sc] = V(lambda v, sc=sc, bp=bp: v.tensor_mul(
            phit[sc % 3], psb[bp][:, 0:512], wrpv[sc % 2]),
            waits=[("t", tv), ("v", wrep_v[sc])])
        if sc >= 2:
            emit_stage2(sc - 1)
            if (sc - 1) % 3 == 2:
                vdg_last, y_last_t = emit_y((sc - 1) // 3)

    emit_stage2(8)
    vdg_last, y_last_t = emit_y(2)
    y_evac_last = y_evac[(2, 15)]

    c_bo = DG(lambda g: g.dma_start(
        out=bcv[:], in_=borow[0, :].partition_broadcast(128)),
        waits=[("dsc", sw_list[8])])
    wo_r = WoT.rearrange("(kt p) o -> p kt o", p=128)
    c_wA1 = DG(lambda g: g.dma_start(
        out=WoA.rearrange("p (kt o) -> p kt o", o=1024)[:, 0:8, :],
        in_=wo_r[:, 0:8, 0:1024]), waits=[("t", m1_last_t)])
    c_wA = DG(lambda g: g.dma_start(
        out=WoA.rearrange("p (kt o) -> p kt o", o=1024)[:, 8:16, :],
        in_=wo_r[:, 8:16, 0:1024]))
    c_wB1 = DG(lambda g: g.dma_start(
        out=WoB.rearrange("p (kt o) -> p kt o", o=1024)[:, 0:8, :],
        in_=wo_r[:, 0:8, 1024:2048]), waits=[("t", y_last_t)])
    c_wB = DG(lambda g: g.dma_start(
        out=WoB.rearrange("p (kt o) -> p kt o", o=1024)[:, 8:16, :],
        in_=wo_r[:, 8:16, 1024:2048]), waits=[("t", y_last_t)])
    p3_evac = {}
    p3_t = {}
    qread = {}
    x2t_read = {}
    slot_sr = {0: None, 1: None}

    tail_v = {}
    prev_xf_v = None
    xf_war = {}
    x2n_written = {}
    tail_cx = {}
    x2n_wr = {}
    tail_va = {}
    tail_aggr2 = {}
    tail_vx2 = {}
    tail_aggr3 = {}
    slot_last = {0: None, 1: None, 2: None, 3: None}

    def lnstat(src_ap, slot, first_waits):
        vb = None
        for q4 in range(4):
            w = list(first_waits)
            if slot_last[slot] is not None:
                w.append(("s", slot_last[slot]))
            vb = V(lambda v, q4=q4, slot=slot, src_ap=src_ap: v.bn_stats(
                out=bnb[:, slot * 24 + q4 * 6: slot * 24 + (q4 + 1) * 6],
                in_=src_ap[:, q4 * 512:(q4 + 1) * 512]),
                waits=(w if q4 == 0 else ()))
        return V(lambda v, slot=slot: v.bn_aggr(
            out=bno[:, slot * 2:slot * 2 + 2],
            in_=bnb[:, slot * 24:(slot + 1) * 24]), waits=[("v", vb)])

    def lnapply_pre(slot, aggr):
        sl = SS(lambda s, slot=slot: s.activation(
            out=tc1[:, slot:slot + 1], in_=bno[:, slot * 2 + 1:slot * 2 + 2],
            func=AF.Ln, bias=epsc[:, 0:1]), waits=[("v", aggr)])
        sr = SS(lambda s, slot=slot: s.activation(
            out=rc2[:, slot:slot + 1], in_=tc1[:, slot:slot + 1], func=AF.Exp,
            scale=-0.5), waits=[("s", sl)])
        vv = V(lambda v, slot=slot: v.scalar_tensor_tensor(
            mc2[:, slot:slot + 1], bno[:, slot * 2:slot * 2 + 1], -1.0,
            rc2[:, slot:slot + 1], ALU.mult, ALU.mult), waits=[("s", sr)])
        return vv

    def tail_A(tt):
        cw_ = [("s", ln1_done[8])]
        if tt >= 2:
            cw_.append(("v", xf_war[tt - 2]))
        if tt == 1:
            cw_.append(("v", q_state["stt_v"]))
        c_x = DH(lambda sy, tt=tt: sy.dma_start(
            out=xfb(tt)[:],
            in_=xin[128 + tt * 128: 128 + (tt + 1) * 128, :]), waits=cw_)
        tail_cx[tt] = c_x
        cxn_w = ([("dsc", x2n_wr[tt - 2])] if tt >= 2 else
                 [("dsc", sw_list[6 + tt])])
        c_xn = DH(lambda sy, tt=tt: sy.dma_start(
            out=XN1S[:, (tt % 2) * 2048:(tt % 2 + 1) * 2048],
            in_=scrx[128 + tt * 128: 128 + (tt + 1) * 128, :]), waits=cxn_w)
        va = V(lambda v, tt=tt: v.tensor_add(
            x2sl(tt), x2sl(tt),
            XN1S[:, (tt % 2) * 2048:(tt % 2 + 1) * 2048]),
            waits=[("dh", c_xn), ("s", p3_evac[(1, tt, 1)])])
        tail_va[tt] = va
        tail_aggr2[tt] = lnstat(x2sl(tt), tt % 2, [("v", va)])

    def tail_B(tt):
        s2 = tt % 2
        vv = lnapply_pre(s2, tail_aggr2[tt])
        szw = [("v", vv)]
        if tt == 0:
            szw += [("v", q_state["stt"][15]), ("v", q_state["stt_v"])]
        if tt >= 1:
            szw.append(("v", tail_vx2[tt - 1]))
        sz = SS(lambda s, tt=tt, s2=s2: s.activation(
            out=z2f, in_=x2sl(tt), func=AF.Identity,
            bias=mc2[:, s2:s2 + 1], scale=rc2[:, s2:s2 + 1]), waits=szw)
        slot_last[s2] = sz
        vx2 = V(lambda v, tt=tt: v.tensor_add(x2sl(tt), xfb(tt)[:], z2f),
                waits=[("s", sz), ("dh", tail_cx[tt])])
        tail_vx2[tt] = vx2
        xf_war[tt] = vx2
        tail_aggr3[tt] = lnstat(x2sl(tt), 2 + tt % 2, [("v", vx2)])

    def tail_C(tt):
        s3 = 2 + tt % 2
        vv = lnapply_pre(s3, tail_aggr3[tt])
        sx2n = SS(lambda s, tt=tt, s3=s3: s.activation(
            out=XN1S[:, (tt % 2) * 2048:(tt % 2 + 1) * 2048],
            in_=x2sl(tt), func=AF.Identity, bias=mc2[:, s3:s3 + 1],
            scale=rc2[:, s3:s3 + 1]), waits=[("v", vv)])
        slot_last[s3] = sx2n
        x2n_written[tt] = sx2n
        x2n_wr[tt] = DS(lambda s_, tt=tt: s_.dma_start(
            out=scr2[tt * 128:(tt + 1) * 128, :],
            in_=XN1S[:, (tt % 2) * 2048:(tt % 2 + 1) * 2048]),
            waits=[("s", sx2n)])

    for oh in range(2):
        c_w = c_wA if oh == 0 else c_wB
        Wv = WoA if oh == 0 else WoB
        for tt in range(8):
            if oh == 1 and tt >= 2:
                tail_C(tt - 2)
            if oh == 1 and tt == 6:
                x2t_read[0] = DH(lambda sy: sy.dma_start(
                    out=x2nT[:].rearrange("p (dt c) -> p dt c", c=OWN)[
                        :, :, 0:512],
                    in_=scr2[0:512, :].rearrange("r (dt p) -> r dt p", p=128),
                    transpose=True),
                    waits=[(("dsc", x2n_wr[3]))] + cover(
                        "dsc", [x2n_wr[i] for i in range(4)]) +
                    [("t", p3_t[(1, 5, 1)])])
            for o5 in range(2):
                bank = (tt * 2 + o5) % 4
                waits = [("dg", c_w)]
                if oh == 0:
                    waits.append(("dg", c_wA1))
                else:
                    waits.append(("dg", c_wB1))
                if oh == 0 and tt == 0:
                    waits += [("s", y_evac_last), ("v", phi_v[8]), ("v", g_v[8])]
                idx = oh * 16 + tt * 2 + o5
                if idx >= 4:
                    pidx = idx - 4
                    p_oh, p_r = pidx // 16, pidx % 16
                    p3w = p3_evac[(p_oh, p_r // 2, p_r % 2)]
                    waits.append(("s", p3w))
                waits += [("dg", c_bo), ("g", g_e1c)]
                for kt in range(16):
                    T(lambda t, oh=oh, tt=tt, o5=o5, kt=kt, bank=bank, Wv=Wv:
                      t.matmul(
                        psb[bank][:, 0:512],
                        yT[:, kt * OWN + tt * 128: kt * OWN + (tt + 1) * 128],
                        Wv[:, kt * 1024 + o5 * 512: kt * 1024 + (o5 + 1) * 512],
                        start=(kt == 0), stop=False),
                        waits=(waits if kt == 0 else ()), inc=False)
                tv = T(lambda t, oh=oh, o5=o5, bank=bank: t.matmul(
                    psb[bank][:, 0:512], E1c[:],
                    bcv[:, oh * 1024 + o5 * 512: oh * 1024 + (o5 + 1) * 512],
                    start=False, stop=True))
                p3_t[(oh, tt, o5)] = tv
                ew = [("t", tv)]
                if oh == 0 and tt == 6 and o5 == 0:
                    ew += [("s", dts_prev), ("v", vdg_last)]
                if oh == 0 and tt == 7 and o5 == 0:
                    ew += [("v", q_state["vq"][14]), ("dg", q_state["scw"][14])]
                if oh == 1 and tt == 7 and o5 == 0:
                    ew += [("v", q_state["vq"][15]), ("dg", q_state["scw"][15])]
                p3_evac[(oh, tt, o5)] = SS(lambda s, oh=oh, tt=tt, o5=o5,
                                           bank=bank: s.activation(
                    out=x2sl(tt)[:, oh * 1024 + o5 * 512: oh * 1024 + (o5 + 1) * 512],
                    in_=psb[bank][:, 0:512], func=AF.Identity), waits=ew)


            if oh == 1 and tt == 4:
                qread[7] = DH(lambda sy: sy.dma_start(
                    out=qTa[:].rearrange("p (k o) -> p k o", o=2048),
                    in_=scrq[:, 0:1024].rearrange("r (k p) -> r k p", p=128),
                    transpose=True),
                    waits=cover("dg", q_state["scw"]) +
                    [("t", p3_t[(0, 7, 1)])])
            if oh == 1:
                if tt >= 1:
                    tail_B(tt - 1)
                tail_A(tt)
    tail_B(7)
    tail_C(6)
    tail_C(7)
    x2t_read[1] = DH(lambda sy: sy.dma_start(
        out=x2nT[:].rearrange("p (dt c) -> p dt c", c=OWN)[:, :, 512:1024],
        in_=scr2[512:1024, :].rearrange("r (dt p) -> r dt p", p=128),
        transpose=True),
        waits=cover("dsc", [x2n_wr[i] for i in range(4, 8)]) +
        [("t", p3_t[(1, 7, 1)])])

    qread[15] = DH(lambda sy: sy.dma_start(
        out=qTb[:].rearrange("p (k o) -> p k o", o=2048),
        in_=scrq[:, 1024:2048].rearrange("r (k p) -> r k p", p=128),
        transpose=True),
        waits=cover("dg", q_state["scw"]) + [("t", p3_t[(1, 7, 1)])])
    c_bf2 = DG(lambda g: g.dma_start(
        out=bcv[:], in_=bfrow[0, :].partition_broadcast(128)),
        waits=[("s", p3_evac[(1, 7, 1)])])

    ffn_evac = {}
    prev_out_v = None
    prev_out_dma = {}
    out_dmas = []

    def fsv(tt):
        return FS if tt % 2 == 0 else XF

    for tt in range(8):
        for o5 in range(4):
            bank = (tt * 4 + o5) % 4
            waits = [("dh", x2t_read[tt // 4]), ("dg", c_bf2)]
            if tt * 4 + o5 >= 4:
                pk = tt * 4 + o5 - 4
                waits.append(("s", ffn_evac[(pk // 4, pk % 4)]))
            if tt == 0:
                waits.append(("s", p3_evac[(1, 7, o5 % 2)]))
            for kt in range(16):
                qv = qTa if kt < 8 else qTb
                kw = ()
                if kt == 0:
                    kw = waits + [("dh", qread[7])]
                elif kt == 8:
                    kw = [("dh", qread[15])]
                T(lambda t, tt=tt, o5=o5, kt=kt, bank=bank, qv=qv: t.matmul(
                    psb[bank][:, 0:512],
                    x2nT[:, kt * OWN + tt * 128: kt * OWN + (tt + 1) * 128],
                    qv[:, (kt % 8) * 2048 + o5 * 512:(kt % 8) * 2048 + (o5 + 1) * 512],
                    start=(kt == 0), stop=False), waits=kw, inc=False)
            tv = T(lambda t, o5=o5, bank=bank: t.matmul(
                psb[bank][:, 0:512], E1c[:], bcv[:, o5 * 512:(o5 + 1) * 512],
                start=False, stop=True))
            ew = [("t", tv)]
            if tt == 0 and o5 == 0:
                ew.append(("v", xf_war[7]))
            if tt == 1 and o5 == 0:
                ew.append(("v", prev_xf_v))
            if tt % 2 in prev_out_dma:
                ew.append(("dh", prev_out_dma[tt % 2]))
            ffn_evac[(tt, o5)] = SS(lambda s, tt=tt, o5=o5, bank=bank:
                                    s.activation(
                out=fsv(tt)[:, o5 * 512:(o5 + 1) * 512],
                in_=psb[bank][:, 0:512], func=AF.Identity), waits=ew)
        p_ = tt % 2
        vb = None
        for q4 in range(4):
            w = [("s", ffn_evac[(tt, 3)])]
            if slot_last[p_] is not None:
                w.append(("s", slot_last[p_]))
            vb = V(lambda v, tt=tt, q4=q4, p_=p_: v.bn_stats(
                out=bnb[:, p_ * 24 + q4 * 6: p_ * 24 + (q4 + 1) * 6],
                in_=fsv(tt)[:, q4 * 512:(q4 + 1) * 512]),
                waits=(w if q4 == 0 else ()))
        va = V(lambda v, p_=p_: v.bn_aggr(
            out=bno[:, p_ * 2:p_ * 2 + 2],
            in_=bnb[:, p_ * 24:(p_ + 1) * 24]), waits=[("v", vb)])
        sl = SS(lambda s, p_=p_: s.activation(
            out=tc1[:, p_:p_ + 1], in_=bno[:, p_ * 2 + 1:p_ * 2 + 2],
            func=AF.Ln, bias=epsc[:, 0:1]), waits=[("v", va)])
        sr = SS(lambda s, tt=tt, p_=p_: s.activation(
            out=rc2[:, p_:p_ + 1], in_=tc1[:, p_:p_ + 1], func=AF.Exp,
            scale=-0.5), waits=[("s", sl)])
        slot_sr[p_] = sr
        vv = V(lambda v, tt=tt, p_=p_: v.scalar_tensor_tensor(
            mc2[:, p_:p_ + 1], bno[:, p_ * 2:p_ * 2 + 1], -1.0,
            rc2[:, p_:p_ + 1], ALU.mult, ALU.mult), waits=[("s", sr)])
        sz = SS(lambda s, tt=tt, p_=p_: s.activation(
            out=z2f, in_=fsv(tt)[:], func=AF.Identity,
            bias=mc2[:, p_:p_ + 1],
            scale=rc2[:, p_:p_ + 1]), waits=[("v", vv)])
        vo = V(lambda v, tt=tt: v.tensor_add(fsv(tt)[:], x2sl(tt), z2f),
               waits=[("s", sz)])
        prev_out_v = vo
        prev_out_dma[tt % 2] = DH(lambda sy, tt=tt: sy.dma_start(
            out=yout[tt * 128:(tt + 1) * 128, :], in_=fsv(tt)[:]),
            waits=[("v", vo)])
        out_dmas.append(prev_out_dma[tt % 2])

    K_dh = P.val("dh") // 16

    def final_wait(sy):
        inst = None
        for i in range(min(NRING, K_dh)):
            tgt = 16 * ((K_dh - 1 - i) // NRING + 1)
            inst = sy.wait_ge(dma_rings["dh"][i], tgt)
        return inst

    P.add("sync", final_wait)

    P_idx = P.dma_idx

    def dma_ring_wait(name, v):
        k = v // 16 - 1
        return dma_rings[name][k % NRING], 16 * (k // NRING + 1)

    with nc.Block() as block:
        def make_body(ename):
            def body(eng):
                for fn, waits, inc in P.ops[ename]:
                    for s_, v_ in waits:
                        if s_ in dma_rings:
                            sem, tgt = dma_ring_wait(s_, v_)
                            eng.wait_ge(sem, tgt)
                        else:
                            eng.wait_ge(sems[s_], v_)
                    if inc is not None and inc[0] in dma_rings:
                        k = P_idx[id(fn)]
                        if k >= NRING:
                            eng.wait_ge(dma_rings[inc[0]][k % NRING],
                                        16 * (k // NRING))
                    inst = fn(eng)
                    if inc is not None and inst is not None:
                        if inc[0] in dma_rings:
                            k = P_idx[id(fn)]
                            inst.then_inc(dma_rings[inc[0]][k % NRING], 16)
                        else:
                            inst.then_inc(sems[inc[0]], inc[1])
            return body

        block.tensor(make_body("tensor"))
        block.vector(make_body("vector"))
        block.scalar(make_body("scalar"))
        block.gpsimd(make_body("gpsimd"))
        block.sync(make_body("sync"))
    es.close()
    return nc


def _host_constants():
    T = 128
    k = np.arange(T)[:, None].astype(np.float64)
    j = np.arange(T)[None, :].astype(np.float64)
    p1 = k - j
    E1 = np.zeros((MT, T, T))
    E2 = np.zeros((MT, T, T))
    fact = 1.0
    for m in range(MT):
        if m > 0:
            fact *= m
        pw = np.where(p1 == 0, 1.0 if m == 0 else 0.0, p1 ** m)
        E1[m] = np.where(p1 >= 0, 2.0 ** (-p1) * pw / fact, 0.0)
        E2[m] = 2.0 ** (-(T + p1)) * pw / fact
    E1T = np.ascontiguousarray(E1.transpose(2, 0, 1).reshape(T, MT * T)).astype(BFD)
    E2T = np.ascontiguousarray(E2.transpose(2, 0, 1).reshape(T, MT * T)).astype(BFD)
    SELh = np.zeros((128, 512), dtype=BFD)
    for blk in range(2):
        for m in range(MT):
            for s in range(S):
                SELh[s, blk * 256 + m * 16 + s] = 1.0
    Ih = np.eye(128, dtype=BFD)
    return E1T, E2T, SELh, Ih


def _col_layout(vec):
    return np.ascontiguousarray(vec.reshape(16, 128).T)


def _ktp_layout(mat):
    s = mat.shape[1]
    return np.ascontiguousarray(
        mat.reshape(16, 128, s).transpose(1, 0, 2).reshape(128, 16 * s))


def kernel(L0_attn_x, L0_attn_proj_in_w, L0_attn_proj_in_b, L0_attn_proj_out_w,
           L0_attn_proj_out_b, L0_attn_A, L0_attn_B_w, L0_attn_C_w,
           L0_attn_D_param, L0_ffn_weight, L0_ffn_bias):
    x = np.ascontiguousarray(np.asarray(L0_attn_x, dtype=np.float32))
    WiTh = np.ascontiguousarray(np.asarray(L0_attn_proj_in_w, np.float32).T)
    WoTh = np.ascontiguousarray(np.asarray(L0_attn_proj_out_w, np.float32).T)
    Wfh = np.ascontiguousarray(np.asarray(L0_ffn_weight, np.float32))
    Ah = _ktp_layout(np.asarray(L0_attn_A, np.float32))
    BwTh = _ktp_layout(np.ascontiguousarray(np.asarray(L0_attn_B_w, np.float32).T))
    CwTh = np.ascontiguousarray(np.asarray(L0_attn_C_w, np.float32).T)
    binh = _col_layout(np.asarray(L0_attn_proj_in_b, np.float32))
    Dh = _col_layout(np.asarray(L0_attn_D_param, np.float32))
    boh = np.asarray(L0_attn_proj_out_b, np.float32).reshape(1, D)
    bfh = np.asarray(L0_ffn_bias, np.float32).reshape(1, D)
    E1T, E2T, SELh, Ih = _host_constants()

    nc = build_program()
    in_maps = []
    for core in range(8):
        b, q = core // 4, core % 4
        if q == 0:
            xin = np.concatenate([np.zeros((128, D), np.float32), x[b, 0:1024]], 0)
            msk = np.ones((128, 9), np.float32)
            msk[:, 0] = 0.0
        else:
            xin = np.ascontiguousarray(x[b, q * 1024 - 128:(q + 1) * 1024])
            msk = np.ones((128, 9), np.float32)
        in_maps.append({
            "xin": xin, "WiT": WiTh, "WoT": WoTh, "Wfn": Wfh, "Arr": Ah,
            "BwT": BwTh, "CwT": CwTh, "binc": binh, "Dcol": Dh, "borow": boh,
            "bfrow": bfh, "E1in": E1T, "E2in": E2T, "SELin": SELh, "Iin": Ih,
            "MSKin": msk,
        })
    res = run_bass_kernel_spmd(nc, in_maps, core_ids=list(range(8)))
    out = np.empty((2, 4096, D), np.float32)
    for core in range(8):
        b, q = core // 4, core % 4
        out[b, q * 1024:(q + 1) * 1024] = res.results[core]["yout"]
    return out


if __name__ == "__main__":
    build_program()
    print("build ok")


# version 28
# speedup vs baseline: 1.1106x; 1.1106x over previous
import numpy as np
import ml_dtypes
from contextlib import ExitStack
import concourse.bass as bass
import concourse.mybir as mybir
from concourse.bass_utils import run_bass_kernel_spmd

F32 = mybir.dt.float32
BF16 = mybir.dt.bfloat16
AF = mybir.ActivationFunctionType
ALU = mybir.AluOpType
BFD = ml_dtypes.bfloat16
AX = mybir.AxisListType.X

D = 2048
S = 16
MT = 16
TOK = 1152
OWN = 1024
NT = 9
NCH = 3
CHW = 384
EPS = 1e-5


class Plan:
    ENGINES = ("tensor", "vector", "scalar", "gpsimd", "sync")

    def __init__(self):
        self.ops = {e: [] for e in self.ENGINES}
        self.cnt = {}
        self.dma_idx = {}

    def val(self, sem):
        return self.cnt.get(sem, 0)

    def add(self, eng, fn, waits=(), inc=None, amt=1):
        rec = None
        if inc is not None:
            self.cnt[inc] = self.cnt.get(inc, 0) + amt
            rec = (inc, amt)
            if amt == 16:
                self.dma_idx[id(fn)] = self.cnt[inc] // 16 - 1
        self.ops[eng].append((fn, tuple(w for w in waits if w and w[1]), rec))
        return self.val(inc) if inc is not None else None


def build_program():
    nc = bass.Bass("TRN2", target_bir_lowering=False)
    es = ExitStack()

    def sb(name, shape, dt):
        return es.enter_context(nc.sbuf_tensor(name, shape, dt))

    def dram(name, shape, dt, kind):
        return nc.dram_tensor(name, shape, dt, kind=kind).ap()

    xin = dram("xin", [TOK, D], F32, "ExternalInput")
    WiT = dram("WiT", [D, D], F32, "ExternalInput")
    WoT = dram("WoT", [D, D], F32, "ExternalInput")
    Wfn = dram("Wfn", [D, D], F32, "ExternalInput")
    Arr = dram("Arr", [128, 256], F32, "ExternalInput")
    BwT = dram("BwT", [128, 256], F32, "ExternalInput")
    CwT = dram("CwT", [S, D], F32, "ExternalInput")
    binc = dram("binc", [128, 16], F32, "ExternalInput")
    Dcol = dram("Dcol", [128, 16], F32, "ExternalInput")
    borow = dram("borow", [1, D], F32, "ExternalInput")
    bfrow = dram("bfrow", [1, D], F32, "ExternalInput")
    E1in = dram("E1in", [128, MT * 128], BF16, "ExternalInput")
    E2in = dram("E2in", [128, MT * 128], BF16, "ExternalInput")
    SELin = dram("SELin", [128, 512], BF16, "ExternalInput")
    Iin = dram("Iin", [128, 128], BF16, "ExternalInput")
    MSKin = dram("MSKin", [128, 9], F32, "ExternalInput")
    yout = dram("yout", [OWN, D], F32, "ExternalOutput")
    scrx = dram("scrx", [TOK, D], BF16, "Internal")
    scr2 = dram("scr2", [OWN, D], BF16, "Internal")
    scrq = dram("scrq", [D, D], BF16, "Internal")

    AR1 = sb("AR1", [128, 18432], BF16)
    AR2 = sb("AR2", [128, 18432], BF16)
    AR3 = sb("AR3", [128, 16384], BF16)
    AR4 = sb("AR4", [128, 17664], BF16)
    WB = sb("WB", [128, 4096], BF16)
    QN = sb("QN", [128, 2048], BF16)
    X2D = sb("X2D", [128, 6144], F32)
    XF = sb("XF", [128, 2048], F32)
    FS = sb("FS", [128, 2048], F32)
    PZ = sb("PZ", [128, 2048], F32)
    XN1S = sb("XN1S", [128, 4096], BF16)

    E1c = sb("E1c", [128, 128], BF16)
    bnb = sb("bnb", [128, 96], F32)
    bno = sb("bno", [128, 8], F32)
    tc1 = sb("tc1", [128, 4], F32)
    r1c = sb("r1c", [128, 9], F32)
    m1c = sb("m1c", [128, 9], F32)
    rc2 = sb("rc2", [128, 4], F32)
    mc2 = sb("mc2", [128, 4], F32)
    thrp = sb("thrp", [128, 16], F32)
    thrn = sb("thrn", [128, 16], F32)
    bincl = sb("bincl", [128, 16], F32)
    Dcl = sb("Dcl", [128, 16], F32)
    mskc = sb("mskc", [128, 9], F32)
    epsc = sb("epsc", [128, 1], F32)
    psb = [es.enter_context(nc.psum_tensor(f"psb{i}", [128, 512], F32))
           for i in range(8)]

    xn1T = AR1
    xprT = AR2
    yT = AR3
    Psi = AR4[:, 0:8192]
    E1T = AR4[:, 8192:10240]
    E2T = AR4[:, 10240:12288]
    SEL = AR4[:, 12288:12800]
    dtp = AR4[:, 12800:13952]
    GTp = AR4[:, 13952:15104]
    wrpv = [AR4[:, 15104:15616], AR4[:, 17152:17664]]
    phit = [AR4[:, 15616 + i * 512: 15616 + (i + 1) * 512] for i in range(3)]
    AR4f = AR4[:].bitcast(F32)
    WBf = WB[:].bitcast(F32)
    WoA = AR1[:, 0:16384]
    WoB = AR2[:, 0:16384]
    qTa = AR1[:, 0:16384]
    qTb = AR2[:, 0:16384]
    x2nT = AR3
    bcv = QN
    X2Db = X2D[:].bitcast(BF16)
    CwTp = X2Db[:, 0:2048]
    qtmv = [X2Db[:, 8192:10240], X2Db[:, 10240:12288]]
    dtsv = [X2D[32 * c:32 * c + 16, 2048:2432] for c in range(3)]
    dts2v = [X2D[32 * c:32 * c + 16, 2432:2816] for c in range(3)]
    Ibf = X2Db[:, 5632:5760]
    ddgr = X2Db[:, 5760:6016]
    BwTs = X2Db[:, 6016:6272]
    z2f = PZ[:, 0:2048]
    XF2 = FS

    def x2sl(tt):
        if tt < 4:
            return AR4f[:, tt * 2048:(tt + 1) * 2048]
        if tt == 4:
            return WBf[:, 0:2048]
        return X2D[:, (tt - 5) * 2048:(tt - 4) * 2048]

    sems = {}
    for sname in ("t", "v", "s", "g"):
        sems[sname] = es.enter_context(nc.semaphore(sname))
    NRING = 8
    dma_rings = {}
    for qname in ("dg", "dh", "dsc"):
        dma_rings[qname] = [es.enter_context(nc.semaphore(f"{qname}{i}"))
                            for i in range(NRING)]

    P = Plan()

    def cover(q, vals):
        best = {}
        for v in vals:
            k = v // 16 - 1
            r = k % 8
            best[r] = max(best.get(r, 0), v)
        return [(q, v) for v in best.values()]

    def DG(fn, waits=()):
        return P.add("gpsimd", fn, waits, "dg", 16)

    def DH(fn, waits=()):
        return P.add("sync", fn, waits, "dh", 16)

    def DS(fn, waits=()):
        return P.add("scalar", fn, waits, "dsc", 16)

    def T(fn, waits=(), inc=True):
        return P.add("tensor", fn, waits, "t" if inc else None)

    def V(fn, waits=(), inc=True):
        return P.add("vector", fn, waits, "v" if inc else None)

    def SS(fn, waits=(), inc=True):
        return P.add("scalar", fn, waits, "s" if inc else None)

    def G(fn, waits=(), inc=True):
        return P.add("gpsimd", fn, waits, "g" if inc else None)

    c_e1 = DG(lambda g: g.dma_start(out=E1T, in_=E1in[:]))
    c_e2 = DG(lambda g: g.dma_start(out=E2T, in_=E2in[:]))
    c_sel = DG(lambda g: g.dma_start(out=SEL, in_=SELin[:]))
    c_i = DG(lambda g: g.dma_start(out=Ibf, in_=Iin[:]))
    c_msk = DG(lambda g: g.dma_start(out=mskc[:], in_=MSKin[:]))
    c_a = DG(lambda g: g.dma_start(out=Af, in_=Arr[:]))
    c_bw = DG(lambda g: g.dma_start(out=BwTs, in_=BwT[:]))
    c_dc = DG(lambda g: g.dma_start(out=Dcl[:], in_=Dcol[:]))
    c_bi = DG(lambda g: g.dma_start(out=bincl[:], in_=binc[:]))
    g_eps = G(lambda g: g.memset(epsc[:], EPS))
    g_z1 = G(lambda g: g.memset(CwTp[:], 0.0))
    g_e0 = G(lambda g: g.memset(E1c[:], 0.0))
    g_e1c = G(lambda g: g.memset(E1c[0:1, :], 1.0), waits=[("g", g_e0)])
    g_z2 = G(lambda g: g.memset(dtp, 0.0))
    g_z3 = G(lambda g: g.memset(GTp, 0.0))
    c_cw = DG(lambda g: g.dma_start(out=CwTp[0:16, :], in_=CwT[:]),
              waits=[("g", g_z1)])

    ln1_done = []
    ln1_slot_sr = {0: None, 1: None}
    sw_list = []
    psb6b = psb[6][:].bitcast(BF16)
    psb7b = psb[7][:].bitcast(BF16)
    psb4b = psb[4][:].bitcast(BF16)
    psb5b = psb[5][:].bitcast(BF16)
    xe = {}
    tr_last = {}

    def xfb(tt):
        return XF if tt % 2 == 0 else XF2

    def xslot(tt):
        i = tt % 3
        return QN[:, 0:2048] if i == 2 else XN1S[:, i * 2048:(i + 1) * 2048]

    def emit_ln1(tt):
        c_x = DH(lambda sy, tt=tt: sy.dma_start(
            out=xfb(tt)[:], in_=xin[tt * 128:(tt + 1) * 128, :]),
            waits=[("s", ln1_done[tt - 2])] if tt >= 2 else [])
        p_ = tt % 2
        vb = None
        for q4 in range(4):
            w = [("dh", c_x)]
            if q4 == 0 and ln1_slot_sr[p_] is not None:
                w.append(("s", ln1_slot_sr[p_]))
            vb = V(lambda v, tt=tt, q4=q4, p_=p_: v.bn_stats(
                out=bnb[:, p_ * 24 + q4 * 6: p_ * 24 + (q4 + 1) * 6],
                in_=xfb(tt)[:, q4 * 512:(q4 + 1) * 512]),
                waits=(w if q4 == 0 else ()))
        va = V(lambda v, p_=p_: v.bn_aggr(out=bno[:, p_ * 2:p_ * 2 + 2],
                                          in_=bnb[:, p_ * 24:(p_ + 1) * 24]),
               waits=[("v", vb)])
        sl = SS(lambda s, p_=p_: s.activation(
            out=tc1[:, p_:p_ + 1], in_=bno[:, p_ * 2 + 1:p_ * 2 + 2],
            func=AF.Ln, bias=epsc[:, 0:1]), waits=[("v", va), ("g", g_eps)])
        sr = SS(lambda s, tt=tt, p_=p_: s.activation(
            out=r1c[:, tt:tt + 1], in_=tc1[:, p_:p_ + 1],
            func=AF.Exp, scale=-0.5), waits=[("s", sl)])
        ln1_slot_sr[p_] = sr
        vv = V(lambda v, tt=tt, p_=p_: v.scalar_tensor_tensor(
            m1c[:, tt:tt + 1], bno[:, p_ * 2:p_ * 2 + 1], -1.0,
            r1c[:, tt:tt + 1], ALU.mult, ALU.mult), waits=[("s", sr)])
        sa_w = [("v", vv)]
        if tt >= 3:
            sa_w += [("dsc", sw_list[tt - 3]), ("t", tr_last[tt - 3])]
        sa = SS(lambda s, tt=tt: s.activation(
            out=xslot(tt), in_=xfb(tt)[:], func=AF.Identity,
            bias=m1c[:, tt:tt + 1], scale=r1c[:, tt:tt + 1]), waits=sa_w)
        ln1_done.append(sa)
        sw = DS(lambda s_, tt=tt: s_.dma_start(
            out=scrx[tt * 128:(tt + 1) * 128, :], in_=xslot(tt)),
            waits=[("s", sa)])
        sw_list.append(sw)

    def emit_xn1_transpose(tt):
        ws = [("s", ln1_done[tt]), ("dg", c_i)]
        if tt >= 1:
            ws += [("v", xe[tt - 1][0]), ("s", xe[tt - 1][1])]
        tvs = []
        for dt_ in range(16):
            pb = psb6b if dt_ < 8 else psb7b
            tv = T(lambda t, tt=tt, dt_=dt_, pb=pb: t.transpose(
                pb[:, (dt_ % 8) * 128:(dt_ % 8 + 1) * 128],
                xslot(tt)[:, dt_ * 128:(dt_ + 1) * 128], Ibf),
                waits=(ws if dt_ == 0 else ()), inc=(dt_ in (7, 15)))
            tvs.append(tv)
        tr_last[tt] = tvs[-1]
        ev_v = V(lambda v, tt=tt: v.tensor_copy(
            out=xn1T[:].rearrange("p (dt c) -> p dt c", c=TOK)[
                :, 0:8, tt * 128:(tt + 1) * 128],
            in_=psb6b[:, 0:1024].rearrange("p (dt c) -> p dt c", c=128)),
            waits=[("t", tvs[7])])
        ev_s = SS(lambda s, tt=tt: s.activation(
            out=xn1T[:].rearrange("p (dt c) -> p dt c", c=TOK)[
                :, 8:16, tt * 128:(tt + 1) * 128],
            in_=psb7b[:, 0:1024].rearrange("p (dt c) -> p dt c", c=128),
            func=AF.Identity), waits=[("t", tvs[15])])
        xe[tt] = (ev_v, ev_s)

    q_state = {"stt": [], "scw": [], "vq": [], "ld": []}

    def emit_quant(ost):
        q = qtmv[ost % 2]
        w_war = ([("v", q_state["vq"][ost - 2])] if ost >= 2
                 else [("s", ln1_done[8])])
        c_s = DH(lambda sy, ost=ost: sy.dma_start(
            out=wfs_view(ost),
            in_=Wfn[ost * 128:(ost + 1) * 128, :]), waits=w_war)
        vr = V(lambda v, ost=ost: v.tensor_reduce(
            out=thrp[:, ost:ost + 1], in_=wfs_view(ost), axis=AX,
            op=ALU.add, apply_absolute_value=True), waits=[("dh", c_s)])
        vr = V(lambda v, ost=ost: v.tensor_scalar(
            thrp[:, ost:ost + 1], thrp[:, ost:ost + 1], 0.7 / 2048.0, None,
            ALU.mult), waits=[("v", vr)])
        vr = V(lambda v, ost=ost: v.tensor_scalar(
            thrn[:, ost:ost + 1], thrp[:, ost:ost + 1], -1.0, None, ALU.mult),
            waits=[("v", vr)])
        gq = V(lambda v, ost=ost, q=q: v.tensor_scalar(
            q, wfs_view(ost), thrn[:, ost:ost + 1], -1.0, ALU.is_lt,
            ALU.mult), waits=[("v", vr)] +
            ([("dg", q_state["scw"][ost - 2])] if ost >= 2 else []))
        vq = V(lambda v, ost=ost, q=q: v.scalar_tensor_tensor(
            q, wfs_view(ost), thrp[:, ost:ost + 1], q,
            ALU.is_gt, ALU.add), waits=[("v", gq)])
        q_state["stt_v"] = vq
        sw = DG(lambda g, ost=ost, q=q: g.dma_start(
            out=scrq[ost * 128:(ost + 1) * 128, :], in_=q),
            waits=[("v", vq)])
        q_state["stt"].append(gq)
        q_state["vq"].append(vq)
        q_state["scw"].append(sw)

    def wfs_view(ost):
        return FS[:] if ost % 2 == 0 else PZ[:]

    quant_emitted = 0


    for tt in range(3):
        emit_ln1(tt)
    for k in range(3):
        emit_xn1_transpose(k)
        emit_ln1(3 + k)
    m1_evac = {}
    m1_t = {}
    wi_r = WiT.rearrange("(kt p) o -> p kt o", p=128)
    for ot in range(16):
        w_war = [("t", m1_t[(ot - 2, 2)])] if ot >= 2 else []
        c_w = DG(lambda g, ot=ot: g.dma_start(
            out=WB[:, (ot % 2) * 2048:(ot % 2 + 1) * 2048].rearrange(
                "p (kt o) -> p kt o", o=128),
            in_=wi_r[:, :, ot * 128:(ot + 1) * 128]), waits=w_war)
        for ch in range(NCH):
            if ot == 0 and ch >= 1:
                for k in range(3):
                    emit_xn1_transpose(3 * ch + k)
                    if ch == 1 and 6 + k < 9:
                        emit_ln1(6 + k)
            bank = (ot % 2) * 3 + ch
            for kt in range(16):
                waits = []
                if kt == 0:
                    waits = [("dg", c_w), ("v", xe[3 * ch + 2][0]),
                             ("s", xe[3 * ch + 2][1])]
                    if (ot - 2, ch) in m1_evac:
                        waits.append(("s", m1_evac[(ot - 2, ch)]))
                tv = T(lambda t, ot=ot, kt=kt, ch=ch, bank=bank: t.matmul(
                    psb[bank][:, 0:CHW],
                    WB[:, (ot % 2) * 2048 + kt * 128:(ot % 2) * 2048 + (kt + 1) * 128],
                    xn1T[:, kt * TOK + ch * CHW: kt * TOK + (ch + 1) * CHW],
                    start=(kt == 0), stop=(kt == 15)),
                    waits=waits, inc=(kt == 15))
                if kt == 15:
                    m1_t[(ot, ch)] = tv
            m1_evac[(ot, ch)] = SS(lambda s, ot=ot, ch=ch, bank=bank: s.activation(
                out=xprT[:, ot * TOK + ch * CHW: ot * TOK + (ch + 1) * CHW],
                in_=psb[bank][:, 0:CHW], func=AF.Identity,
                bias=bincl[:, ot:ot + 1]),
                waits=[("t", m1_t[(ot, ch)]), ("dg", c_bi)])
        if ot >= 4 and quant_emitted < 16:
            emit_quant(quant_emitted)
            quant_emitted += 1
            if ot >= 9 and quant_emitted < 16:
                emit_quant(quant_emitted)
                quant_emitted += 1
    while quant_emitted < 16:
        emit_quant(quant_emitted)
        quant_emitted += 1
    m1_last_t = m1_t[(15, 2)]
    m1_last_s = m1_evac[(15, 2)]

    v0 = V(lambda v: v.tensor_mul(tAv, Af, Af), waits=[("dg", c_a)])
    v1 = V(lambda v: v.tensor_scalar(tBv, tAv, -1.0 / 2880.0, 1.0 / 192.0,
                                     ALU.mult, ALU.add), waits=[("v", v0)])
    v2 = V(lambda v: v.tensor_mul(tBv, tBv, tAv), waits=[("v", v1)])
    v3 = V(lambda v: v.tensor_scalar(tBv, tBv, -1.0 / 8.0, None, ALU.add),
           waits=[("v", v2)])
    v4 = V(lambda v: v.tensor_mul(tBv, tBv, tAv), waits=[("v", v3)])
    v5 = V(lambda v: v.scalar_tensor_tensor(psiv, Af, 0.5, tBv,
                                            ALU.mult, ALU.add), waits=[("v", v4)])
    s_beta = SS(lambda s: s.activation(out=betav, in_=psiv, func=AF.Exp,
                                       scale=128.0), waits=[("v", v5)])
    AR3f = AR3[:].bitcast(F32)
    Af = AR3f[:, 4096:4352]
    psiv = AR3f[:, 4352:4608]
    betav = AR3f[:, 4608:4864]
    tAv = AR3f[:, 4864:5120]
    tBv = AR3f[:, 5120:5376]

    def fsm(m):
        return AR3f[:, m * 256:(m + 1) * 256]

    ST4 = AR3f[:, 0:4096].rearrange("p (m dt s) -> p m dt s", m=MT, dt=16)
    pview = Psi.rearrange("p (dt r) -> p dt r", r=512)
    d1 = pview[:, :, 0:256].rearrange("p dt (m s) -> p m dt s", s=16)
    d2 = pview[:, :, 256:512].rearrange("p dt (m s) -> p m dt s", s=16)
    vm = V(lambda v: v.memset(fsm(0), 1.0), waits=[("v", v5)])
    vm = V(lambda v: v.tensor_copy(out=fsm(1), in_=psiv), waits=[("v", vm)])
    for m in range(2, MT):
        vm = V(lambda v, m=m: v.tensor_mul(fsm(m), fsm(m - 1), psiv),
               waits=[("v", vm)])
    vm = V(lambda v: v.tensor_copy(out=d1, in_=ST4), waits=[("v", vm)])
    for m in range(MT):
        vm = V(lambda v, m=m: v.tensor_mul(fsm(m), fsm(m), betav),
               waits=[("v", vm), ("s", s_beta)])
    vm = V(lambda v: v.tensor_copy(out=d2, in_=ST4), waits=[("v", vm)])
    v_psi = vm

    phi_v = {}
    g_v = {}
    wrep_v = {}
    st2_t = {}
    st2_prev_bank_v = {}
    y_evac = {}
    y_t_war = {}
    y_last_t = None
    vdg_last = None

    v_dt = {}
    sp_s4 = {}
    t_delta = {}
    for ch in range(NCH):
        bank = 6 if ch != 1 else 7
        waits = [("s", m1_last_s), ("dg", c_bw), ("v", xe[8][0]),
                 ("s", xe[8][1])]
        if ch == 2:
            waits.append(("s", sp_s4[0]))
        for dt_ in range(16):
            tv = T(lambda t, ch=ch, dt_=dt_, bank=bank: t.matmul(
                psb[bank][0:16, 0:CHW], BwTs[:, dt_ * 16:(dt_ + 1) * 16],
                xprT[:, dt_ * TOK + ch * CHW: dt_ * TOK + (ch + 1) * CHW],
                start=(dt_ == 0), stop=(dt_ == 15)),
                waits=(waits if dt_ == 0 else ()), inc=(dt_ == 15))
        t_delta[ch] = tv
        dts, dts2 = dtsv[ch], dts2v[ch]
        s1 = SS(lambda s, ch=ch, bank=bank, dts=dts: s.activation(
            out=dts, in_=psb[bank][0:16, 0:CHW], func=AF.Abs),
            waits=[("t", tv)])
        s2 = SS(lambda s, dts=dts: s.activation(out=dts, in_=dts, func=AF.Exp,
                                                scale=-1.0), waits=[("s", s1)])
        s3 = SS(lambda s, dts=dts: s.activation(out=dts, in_=dts, func=AF.Ln,
                                                bias=1.0), waits=[("s", s2)])
        s4 = SS(lambda s, ch=ch, bank=bank, dts2=dts2: s.activation(
            out=dts2, in_=psb[bank][0:16, 0:CHW], func=AF.Relu),
            waits=[("s", s3)])
        sp_s4[ch] = s4
        v_dt[ch] = V(lambda v, ch=ch, dts=dts, dts2=dts2: v.tensor_add(
            dtp[0:16, ch * CHW:(ch + 1) * CHW], dts, dts2),
            waits=[("s", s4), ("g", g_z2)])
    dts_prev = sp_s4[2]

    def emit_stage2(sc):
        bg = 4 + sc % 2
        waits0 = [("v", phi_v[sc]), ("v", phi_v[sc - 1]),
                  ("dg", c_e1), ("dg", c_e2)]
        if sc <= 2:
            waits0.append(("s", m1_last_s))
        if bg in st2_prev_bank_v:
            waits0.append(st2_prev_bank_v[bg])
        n = 0
        tv = None
        for blk in range(2):
            srcp = phit[(sc - 1) % 3] if blk == 0 else phit[sc % 3]
            ET = E2T if blk == 0 else E1T
            off = 256 if blk == 0 else 0
            for m in range(MT):
                tv = T(lambda t, srcp=srcp, ET=ET, off=off, m=m, bg=bg, n=n:
                       t.matmul(psb[bg][0:16, 0:128],
                                srcp[:, off + m * 16: off + (m + 1) * 16],
                                ET[:, m * 128:(m + 1) * 128],
                                start=(n == 0), stop=(n == 2 * MT - 1)),
                       waits=(waits0 if n == 0 else ()),
                       inc=(n == 2 * MT - 1))
                n += 1
        st2_t[sc] = tv
        g_v[sc] = V(lambda v, sc=sc, bg=bg: v.tensor_copy(
            out=GTp[0:16, sc * 128:(sc + 1) * 128],
            in_=psb[bg][0:16, 0:128]), waits=[("t", tv), ("g", g_z3)])
        st2_prev_bank_v[bg] = ("v", g_v[sc])

    def emit_y(ch):
        nonlocal_vals = {}
        t0 = ch * CHW if ch > 0 else 128
        n_y = (ch + 1) * CHW - t0
        sc_hi = ch * 3 + 2
        for dt_ in range(16):
            bank = 4 + dt_ % 2
            dslot = dt_ % 2
            dw = [("dg", c_i), ("dg", c_dc)]
            if (ch, dt_ - 2) in y_evac:
                dw.append(("t", y_t_war[dt_ - 2]))
            elif ch > 0:
                dw.append(("t", y_t_war[14 + dslot]))
            vdg = V(lambda v, dt_=dt_, dslot=dslot: v.tensor_scalar(
                ddgr[:, dslot * 128:(dslot + 1) * 128], Ibf,
                Dcl[:, dt_:dt_ + 1], None, ALU.mult), waits=dw)
            waits = [("v", g_v[sc_hi]), ("dg", c_cw)]
            if (ch, dt_ - 2) in y_evac:
                waits.append(("s", y_evac[(ch, dt_ - 2)]))
            elif bank in st2_prev_bank_v and dt_ < 2:
                waits.append(st2_prev_bank_v[bank])
            T(lambda t, dt_=dt_, bank=bank, t0=t0, n_y=n_y: t.matmul(
                psb[bank][:, 0:n_y], CwTp[:, dt_ * 128:(dt_ + 1) * 128],
                GTp[:, t0:t0 + n_y], start=True, stop=False),
                waits=waits, inc=False)
            tv = T(lambda t, dt_=dt_, bank=bank, t0=t0, n_y=n_y, dslot=dslot:
                   t.matmul(
                psb[bank][:, 0:n_y], ddgr[:, dslot * 128:(dslot + 1) * 128],
                xprT[:, dt_ * TOK + t0: dt_ * TOK + t0 + n_y],
                start=False, stop=True), waits=[("v", vdg)])
            y_t_war[dt_] = tv
            y_evac[(ch, dt_)] = SS(lambda s, dt_=dt_, bank=bank, t0=t0, n_y=n_y:
                                   s.activation(
                out=yT[:, dt_ * OWN + t0 - 128: dt_ * OWN + t0 - 128 + n_y],
                in_=psb[bank][:, 0:n_y], func=AF.Identity), waits=[("t", tv)])
        for bank in (4, 5):
            st2_prev_bank_v[bank] = ("s", y_evac[(ch, 14 + (bank - 4))])
        return vdg, tv

    for sc in range(9):
        ch = sc // 3
        bp, bw_ = sc % 2, 2 + sc % 2
        waits = [("v", v_dt[ch]), ("dg", c_sel), ("dg", c_msk)]
        if sc >= 2:
            waits.append(("v", phi_v[sc - 2]))
        t_wr = T(lambda t, sc=sc, bw_=bw_: t.matmul(
            psb[bw_][:, 0:512], dtp[:, sc * 128:(sc + 1) * 128], SEL,
            start=True, stop=True), waits=waits)
        wrep_v[sc] = V(lambda v, sc=sc, bw_=bw_: v.tensor_scalar(
            wrpv[sc % 2], psb[bw_][:, 0:512], mskc[:, sc:sc + 1], None,
            ALU.mult), waits=[("t", t_wr)])
        for dt_ in range(16):
            waits = []
            if dt_ == 0:
                waits = [("s", m1_last_s), ("v", v_psi)]
                if sc >= 2:
                    waits.append(("v", phi_v[sc - 2]))
            tv = T(lambda t, sc=sc, dt_=dt_, bp=bp: t.matmul(
                psb[bp][:, 0:512],
                xprT[:, dt_ * TOK + sc * 128: dt_ * TOK + (sc + 1) * 128],
                Psi[:, dt_ * 512:(dt_ + 1) * 512],
                start=(dt_ == 0), stop=(dt_ == 15)),
                waits=waits, inc=(dt_ == 15))
        phi_v[sc] = V(lambda v, sc=sc, bp=bp: v.tensor_mul(
            phit[sc % 3], psb[bp][:, 0:512], wrpv[sc % 2]),
            waits=[("t", tv), ("v", wrep_v[sc])])
        if sc >= 2:
            emit_stage2(sc - 1)
            if (sc - 1) % 3 == 2:
                vdg_last, y_last_t = emit_y((sc - 1) // 3)

    emit_stage2(8)
    vdg_last, y_last_t = emit_y(2)
    y_evac_last = y_evac[(2, 15)]

    c_bo = DG(lambda g: g.dma_start(
        out=bcv[:], in_=borow[0, :].partition_broadcast(128)),
        waits=[("t", tr_last[8]), ("dsc", sw_list[8])])
    wo_r = WoT.rearrange("(kt p) o -> p kt o", p=128)
    c_wA1 = DG(lambda g: g.dma_start(
        out=WoA.rearrange("p (kt o) -> p kt o", o=1024)[:, 0:8, :],
        in_=wo_r[:, 0:8, 0:1024]), waits=[("t", m1_last_t)])
    c_wA = DG(lambda g: g.dma_start(
        out=WoA.rearrange("p (kt o) -> p kt o", o=1024)[:, 8:16, :],
        in_=wo_r[:, 8:16, 0:1024]))
    c_wB1 = DG(lambda g: g.dma_start(
        out=WoB.rearrange("p (kt o) -> p kt o", o=1024)[:, 0:8, :],
        in_=wo_r[:, 0:8, 1024:2048]), waits=[("t", y_last_t)])
    c_wB = DG(lambda g: g.dma_start(
        out=WoB.rearrange("p (kt o) -> p kt o", o=1024)[:, 8:16, :],
        in_=wo_r[:, 8:16, 1024:2048]), waits=[("t", y_last_t)])
    p3_evac = {}
    p3_t = {}
    qread = {}
    x2t_read = {}
    slot_sr = {0: None, 1: None}

    tail_v = {}
    prev_xf_v = None
    xf_war = {}
    x2n_written = {}
    tail_cx = {}
    x2n_wr = {}
    tail_va = {}
    tail_aggr2 = {}
    tail_vx2 = {}
    tail_aggr3 = {}
    slot_last = {0: None, 1: None, 2: None, 3: None}

    def lnstat(src_ap, slot, first_waits):
        vb = None
        for q4 in range(4):
            w = list(first_waits)
            if slot_last[slot] is not None:
                w.append(("s", slot_last[slot]))
            vb = V(lambda v, q4=q4, slot=slot, src_ap=src_ap: v.bn_stats(
                out=bnb[:, slot * 24 + q4 * 6: slot * 24 + (q4 + 1) * 6],
                in_=src_ap[:, q4 * 512:(q4 + 1) * 512]),
                waits=(w if q4 == 0 else ()))
        return V(lambda v, slot=slot: v.bn_aggr(
            out=bno[:, slot * 2:slot * 2 + 2],
            in_=bnb[:, slot * 24:(slot + 1) * 24]), waits=[("v", vb)])

    def lnapply_pre(slot, aggr):
        sl = SS(lambda s, slot=slot: s.activation(
            out=tc1[:, slot:slot + 1], in_=bno[:, slot * 2 + 1:slot * 2 + 2],
            func=AF.Ln, bias=epsc[:, 0:1]), waits=[("v", aggr)])
        sr = SS(lambda s, slot=slot: s.activation(
            out=rc2[:, slot:slot + 1], in_=tc1[:, slot:slot + 1], func=AF.Exp,
            scale=-0.5), waits=[("s", sl)])
        vv = V(lambda v, slot=slot: v.scalar_tensor_tensor(
            mc2[:, slot:slot + 1], bno[:, slot * 2:slot * 2 + 1], -1.0,
            rc2[:, slot:slot + 1], ALU.mult, ALU.mult), waits=[("s", sr)])
        return vv

    def tail_A(tt):
        cw_ = [("s", ln1_done[8])]
        if tt >= 2:
            cw_.append(("v", xf_war[tt - 2]))
        if tt == 1:
            cw_.append(("v", q_state["stt_v"]))
        c_x = DH(lambda sy, tt=tt: sy.dma_start(
            out=xfb(tt)[:],
            in_=xin[128 + tt * 128: 128 + (tt + 1) * 128, :]), waits=cw_)
        tail_cx[tt] = c_x
        cxn_w = ([("dsc", x2n_wr[tt - 2])] if tt >= 2 else
                 [("dsc", sw_list[6 + tt]), ("t", tr_last[6 + tt])])
        c_xn = DH(lambda sy, tt=tt: sy.dma_start(
            out=XN1S[:, (tt % 2) * 2048:(tt % 2 + 1) * 2048],
            in_=scrx[128 + tt * 128: 128 + (tt + 1) * 128, :]), waits=cxn_w)
        va = V(lambda v, tt=tt: v.tensor_add(
            x2sl(tt), x2sl(tt),
            XN1S[:, (tt % 2) * 2048:(tt % 2 + 1) * 2048]),
            waits=[("dh", c_xn), ("s", p3_evac[(1, tt, 1)])])
        tail_va[tt] = va
        tail_aggr2[tt] = lnstat(x2sl(tt), tt % 2, [("v", va)])

    def tail_B(tt):
        s2 = tt % 2
        vv = lnapply_pre(s2, tail_aggr2[tt])
        szw = [("v", vv)]
        if tt == 0:
            szw += [("v", q_state["stt"][15]), ("v", q_state["stt_v"])]
        if tt >= 1:
            szw.append(("v", tail_vx2[tt - 1]))
        sz = SS(lambda s, tt=tt, s2=s2: s.activation(
            out=z2f, in_=x2sl(tt), func=AF.Identity,
            bias=mc2[:, s2:s2 + 1], scale=rc2[:, s2:s2 + 1]), waits=szw)
        slot_last[s2] = sz
        vx2 = V(lambda v, tt=tt: v.tensor_add(x2sl(tt), xfb(tt)[:], z2f),
                waits=[("s", sz), ("dh", tail_cx[tt])])
        tail_vx2[tt] = vx2
        xf_war[tt] = vx2
        tail_aggr3[tt] = lnstat(x2sl(tt), 2 + tt % 2, [("v", vx2)])

    def tail_C(tt):
        s3 = 2 + tt % 2
        vv = lnapply_pre(s3, tail_aggr3[tt])
        sx2n = SS(lambda s, tt=tt, s3=s3: s.activation(
            out=XN1S[:, (tt % 2) * 2048:(tt % 2 + 1) * 2048],
            in_=x2sl(tt), func=AF.Identity, bias=mc2[:, s3:s3 + 1],
            scale=rc2[:, s3:s3 + 1]), waits=[("v", vv)])
        slot_last[s3] = sx2n
        x2n_written[tt] = sx2n
        x2n_wr[tt] = DS(lambda s_, tt=tt: s_.dma_start(
            out=scr2[tt * 128:(tt + 1) * 128, :],
            in_=XN1S[:, (tt % 2) * 2048:(tt % 2 + 1) * 2048]),
            waits=[("s", sx2n)])

    for oh in range(2):
        c_w = c_wA if oh == 0 else c_wB
        Wv = WoA if oh == 0 else WoB
        for tt in range(8):
            if oh == 1 and tt >= 2:
                tail_C(tt - 2)
            if oh == 1 and tt == 6:
                x2t_read[0] = DH(lambda sy: sy.dma_start(
                    out=x2nT[:].rearrange("p (dt c) -> p dt c", c=OWN)[
                        :, :, 0:512],
                    in_=scr2[0:512, :].rearrange("r (dt p) -> r dt p", p=128),
                    transpose=True),
                    waits=[(("dsc", x2n_wr[3]))] + cover(
                        "dsc", [x2n_wr[i] for i in range(4)]) +
                    [("t", p3_t[(1, 5, 1)])])
            for o5 in range(2):
                bank = (tt * 2 + o5) % 4
                waits = [("dg", c_w)]
                if oh == 0:
                    waits.append(("dg", c_wA1))
                else:
                    waits.append(("dg", c_wB1))
                if oh == 0 and tt == 0:
                    waits += [("s", y_evac_last), ("v", phi_v[8]), ("v", g_v[8])]
                idx = oh * 16 + tt * 2 + o5
                if idx >= 4:
                    pidx = idx - 4
                    p_oh, p_r = pidx // 16, pidx % 16
                    p3w = p3_evac[(p_oh, p_r // 2, p_r % 2)]
                    waits.append(("s", p3w))
                waits += [("dg", c_bo), ("g", g_e1c)]
                for kt in range(16):
                    T(lambda t, oh=oh, tt=tt, o5=o5, kt=kt, bank=bank, Wv=Wv:
                      t.matmul(
                        psb[bank][:, 0:512],
                        yT[:, kt * OWN + tt * 128: kt * OWN + (tt + 1) * 128],
                        Wv[:, kt * 1024 + o5 * 512: kt * 1024 + (o5 + 1) * 512],
                        start=(kt == 0), stop=False),
                        waits=(waits if kt == 0 else ()), inc=False)
                tv = T(lambda t, oh=oh, o5=o5, bank=bank: t.matmul(
                    psb[bank][:, 0:512], E1c[:],
                    bcv[:, oh * 1024 + o5 * 512: oh * 1024 + (o5 + 1) * 512],
                    start=False, stop=True))
                p3_t[(oh, tt, o5)] = tv
                ew = [("t", tv)]
                if oh == 0 and tt == 6 and o5 == 0:
                    ew += [("s", dts_prev), ("v", vdg_last)]
                if oh == 0 and tt == 7 and o5 == 0:
                    ew += [("v", q_state["vq"][14]), ("dg", q_state["scw"][14])]
                if oh == 1 and tt == 7 and o5 == 0:
                    ew += [("v", q_state["vq"][15]), ("dg", q_state["scw"][15])]
                p3_evac[(oh, tt, o5)] = SS(lambda s, oh=oh, tt=tt, o5=o5,
                                           bank=bank: s.activation(
                    out=x2sl(tt)[:, oh * 1024 + o5 * 512: oh * 1024 + (o5 + 1) * 512],
                    in_=psb[bank][:, 0:512], func=AF.Identity), waits=ew)


            if oh == 1 and tt == 4:
                qread[7] = DH(lambda sy: sy.dma_start(
                    out=qTa[:].rearrange("p (k o) -> p k o", o=2048),
                    in_=scrq[:, 0:1024].rearrange("r (k p) -> r k p", p=128),
                    transpose=True),
                    waits=cover("dg", q_state["scw"]) +
                    [("t", p3_t[(0, 7, 1)])])
            if oh == 1:
                if tt >= 1:
                    tail_B(tt - 1)
                tail_A(tt)
    tail_B(7)
    tail_C(6)
    tail_C(7)
    qread[15] = DH(lambda sy: sy.dma_start(
        out=qTb[:].rearrange("p (k o) -> p k o", o=2048),
        in_=scrq[:, 1024:2048].rearrange("r (k p) -> r k p", p=128),
        transpose=True),
        waits=cover("dg", q_state["scw"]) + [("t", p3_t[(1, 7, 1)])])
    x2t_read[1] = DH(lambda sy: sy.dma_start(
        out=x2nT[:].rearrange("p (dt c) -> p dt c", c=OWN)[:, :, 512:1024],
        in_=scr2[512:1024, :].rearrange("r (dt p) -> r dt p", p=128),
        transpose=True),
        waits=cover("dsc", [x2n_wr[i] for i in range(4, 8)]) +
        [("t", p3_t[(1, 7, 1)])])

    c_bf2 = DG(lambda g: g.dma_start(
        out=bcv[:], in_=bfrow[0, :].partition_broadcast(128)),
        waits=[("s", p3_evac[(1, 7, 1)])])

    ffn_evac = {}
    prev_out_v = None
    prev_out_dma = {}
    out_dmas = []

    def fsv(tt):
        return FS if tt % 2 == 0 else XF

    for tt in range(8):
        for o5 in range(4):
            bank = (tt * 4 + o5) % 4
            waits = [("dh", x2t_read[tt // 4]), ("dg", c_bf2)]
            if tt * 4 + o5 >= 4:
                pk = tt * 4 + o5 - 4
                waits.append(("s", ffn_evac[(pk // 4, pk % 4)]))
            if tt == 0:
                waits.append(("s", p3_evac[(1, 7, o5 % 2)]))
            for kt in range(16):
                qv = qTa if kt < 8 else qTb
                kw = ()
                if kt == 0:
                    kw = waits + [("dh", qread[7])]
                elif kt == 8:
                    kw = [("dh", qread[15])]
                T(lambda t, tt=tt, o5=o5, kt=kt, bank=bank, qv=qv: t.matmul(
                    psb[bank][:, 0:512],
                    x2nT[:, kt * OWN + tt * 128: kt * OWN + (tt + 1) * 128],
                    qv[:, (kt % 8) * 2048 + o5 * 512:(kt % 8) * 2048 + (o5 + 1) * 512],
                    start=(kt == 0), stop=False), waits=kw, inc=False)
            tv = T(lambda t, o5=o5, bank=bank: t.matmul(
                psb[bank][:, 0:512], E1c[:], bcv[:, o5 * 512:(o5 + 1) * 512],
                start=False, stop=True))
            ew = [("t", tv)]
            if tt == 0 and o5 == 0:
                ew.append(("v", xf_war[7]))
            if tt == 1 and o5 == 0:
                ew.append(("v", prev_xf_v))
            if tt % 2 in prev_out_dma:
                ew.append(("dh", prev_out_dma[tt % 2]))
            ffn_evac[(tt, o5)] = SS(lambda s, tt=tt, o5=o5, bank=bank:
                                    s.activation(
                out=fsv(tt)[:, o5 * 512:(o5 + 1) * 512],
                in_=psb[bank][:, 0:512], func=AF.Identity), waits=ew)
        p_ = tt % 2
        vb = None
        for q4 in range(4):
            w = [("s", ffn_evac[(tt, 3)])]
            if slot_last[p_] is not None:
                w.append(("s", slot_last[p_]))
            vb = V(lambda v, tt=tt, q4=q4, p_=p_: v.bn_stats(
                out=bnb[:, p_ * 24 + q4 * 6: p_ * 24 + (q4 + 1) * 6],
                in_=fsv(tt)[:, q4 * 512:(q4 + 1) * 512]),
                waits=(w if q4 == 0 else ()))
        va = V(lambda v, p_=p_: v.bn_aggr(
            out=bno[:, p_ * 2:p_ * 2 + 2],
            in_=bnb[:, p_ * 24:(p_ + 1) * 24]), waits=[("v", vb)])
        sl = SS(lambda s, p_=p_: s.activation(
            out=tc1[:, p_:p_ + 1], in_=bno[:, p_ * 2 + 1:p_ * 2 + 2],
            func=AF.Ln, bias=epsc[:, 0:1]), waits=[("v", va)])
        sr = SS(lambda s, tt=tt, p_=p_: s.activation(
            out=rc2[:, p_:p_ + 1], in_=tc1[:, p_:p_ + 1], func=AF.Exp,
            scale=-0.5), waits=[("s", sl)])
        slot_sr[p_] = sr
        vv = V(lambda v, tt=tt, p_=p_: v.scalar_tensor_tensor(
            mc2[:, p_:p_ + 1], bno[:, p_ * 2:p_ * 2 + 1], -1.0,
            rc2[:, p_:p_ + 1], ALU.mult, ALU.mult), waits=[("s", sr)])
        sz = SS(lambda s, tt=tt, p_=p_: s.activation(
            out=z2f, in_=fsv(tt)[:], func=AF.Identity,
            bias=mc2[:, p_:p_ + 1],
            scale=rc2[:, p_:p_ + 1]), waits=[("v", vv)])
        vo = V(lambda v, tt=tt: v.tensor_add(fsv(tt)[:], x2sl(tt), z2f),
               waits=[("s", sz)])
        prev_out_v = vo
        prev_out_dma[tt % 2] = DH(lambda sy, tt=tt: sy.dma_start(
            out=yout[tt * 128:(tt + 1) * 128, :], in_=fsv(tt)[:]),
            waits=[("v", vo)])
        out_dmas.append(prev_out_dma[tt % 2])

    K_dh = P.val("dh") // 16

    def final_wait(sy):
        inst = None
        for i in range(min(NRING, K_dh)):
            tgt = 16 * ((K_dh - 1 - i) // NRING + 1)
            inst = sy.wait_ge(dma_rings["dh"][i], tgt)
        return inst

    P.add("sync", final_wait)

    P_idx = P.dma_idx

    def dma_ring_wait(name, v):
        k = v // 16 - 1
        return dma_rings[name][k % NRING], 16 * (k // NRING + 1)

    with nc.Block() as block:
        def make_body(ename):
            def body(eng):
                for fn, waits, inc in P.ops[ename]:
                    for s_, v_ in waits:
                        if s_ in dma_rings:
                            sem, tgt = dma_ring_wait(s_, v_)
                            eng.wait_ge(sem, tgt)
                        else:
                            eng.wait_ge(sems[s_], v_)
                    if inc is not None and inc[0] in dma_rings:
                        k = P_idx[id(fn)]
                        if k >= NRING:
                            eng.wait_ge(dma_rings[inc[0]][k % NRING],
                                        16 * (k // NRING))
                    inst = fn(eng)
                    if inc is not None and inst is not None:
                        if inc[0] in dma_rings:
                            k = P_idx[id(fn)]
                            inst.then_inc(dma_rings[inc[0]][k % NRING], 16)
                        else:
                            inst.then_inc(sems[inc[0]], inc[1])
            return body

        block.tensor(make_body("tensor"))
        block.vector(make_body("vector"))
        block.scalar(make_body("scalar"))
        block.gpsimd(make_body("gpsimd"))
        block.sync(make_body("sync"))
    es.close()
    return nc


def _host_constants():
    T = 128
    k = np.arange(T)[:, None].astype(np.float64)
    j = np.arange(T)[None, :].astype(np.float64)
    p1 = k - j
    E1 = np.zeros((MT, T, T))
    E2 = np.zeros((MT, T, T))
    fact = 1.0
    for m in range(MT):
        if m > 0:
            fact *= m
        pw = np.where(p1 == 0, 1.0 if m == 0 else 0.0, p1 ** m)
        E1[m] = np.where(p1 >= 0, 2.0 ** (-p1) * pw / fact, 0.0)
        E2[m] = 2.0 ** (-(T + p1)) * pw / fact
    E1T = np.ascontiguousarray(E1.transpose(2, 0, 1).reshape(T, MT * T)).astype(BFD)
    E2T = np.ascontiguousarray(E2.transpose(2, 0, 1).reshape(T, MT * T)).astype(BFD)
    SELh = np.zeros((128, 512), dtype=BFD)
    for blk in range(2):
        for m in range(MT):
            for s in range(S):
                SELh[s, blk * 256 + m * 16 + s] = 1.0
    Ih = np.eye(128, dtype=BFD)
    return E1T, E2T, SELh, Ih


def _col_layout(vec):
    return np.ascontiguousarray(vec.reshape(16, 128).T)


def _ktp_layout(mat):
    s = mat.shape[1]
    return np.ascontiguousarray(
        mat.reshape(16, 128, s).transpose(1, 0, 2).reshape(128, 16 * s))


def kernel(L0_attn_x, L0_attn_proj_in_w, L0_attn_proj_in_b, L0_attn_proj_out_w,
           L0_attn_proj_out_b, L0_attn_A, L0_attn_B_w, L0_attn_C_w,
           L0_attn_D_param, L0_ffn_weight, L0_ffn_bias):
    x = np.ascontiguousarray(np.asarray(L0_attn_x, dtype=np.float32))
    WiTh = np.ascontiguousarray(np.asarray(L0_attn_proj_in_w, np.float32).T)
    WoTh = np.ascontiguousarray(np.asarray(L0_attn_proj_out_w, np.float32).T)
    Wfh = np.ascontiguousarray(np.asarray(L0_ffn_weight, np.float32))
    Ah = _ktp_layout(np.asarray(L0_attn_A, np.float32))
    BwTh = _ktp_layout(np.ascontiguousarray(np.asarray(L0_attn_B_w, np.float32).T))
    CwTh = np.ascontiguousarray(np.asarray(L0_attn_C_w, np.float32).T)
    binh = _col_layout(np.asarray(L0_attn_proj_in_b, np.float32))
    Dh = _col_layout(np.asarray(L0_attn_D_param, np.float32))
    boh = np.asarray(L0_attn_proj_out_b, np.float32).reshape(1, D)
    bfh = np.asarray(L0_ffn_bias, np.float32).reshape(1, D)
    E1T, E2T, SELh, Ih = _host_constants()

    nc = build_program()
    in_maps = []
    for core in range(8):
        b, q = core // 4, core % 4
        if q == 0:
            xin = np.concatenate([np.zeros((128, D), np.float32), x[b, 0:1024]], 0)
            msk = np.ones((128, 9), np.float32)
            msk[:, 0] = 0.0
        else:
            xin = np.ascontiguousarray(x[b, q * 1024 - 128:(q + 1) * 1024])
            msk = np.ones((128, 9), np.float32)
        in_maps.append({
            "xin": xin, "WiT": WiTh, "WoT": WoTh, "Wfn": Wfh, "Arr": Ah,
            "BwT": BwTh, "CwT": CwTh, "binc": binh, "Dcol": Dh, "borow": boh,
            "bfrow": bfh, "E1in": E1T, "E2in": E2T, "SELin": SELh, "Iin": Ih,
            "MSKin": msk,
        })
    res = run_bass_kernel_spmd(nc, in_maps, core_ids=list(range(8)))
    out = np.empty((2, 4096, D), np.float32)
    for core in range(8):
        b, q = core // 4, core % 4
        out[b, q * 1024:(q + 1) * 1024] = res.results[core]["yout"]
    return out


if __name__ == "__main__":
    build_program()
    print("build ok")
